# version 05
# baseline (speedup 1.0000x reference)
import numpy as np
from contextlib import ExitStack

import concourse.bass as bass
import concourse.mybir as mybir
from concourse.bass_utils import run_bass_kernel_spmd

F32 = mybir.dt.float32
I32 = mybir.dt.int32
AF = mybir.ActivationFunctionType
ALU = mybir.AluOpType
AX = mybir.AxisListType

D = 1024
DIN = 4384
INTER = 2048
STATE = 128
H = 32
P = 64
KCONV = 4
QCH = 128
NCH = 16
T = 2048
HALO = 4
TH = T + HALO
I_FFN = 2816
DK = 8
ICH = 22

ENGS = ("sync", "tensor", "vector", "scalar", "gpsimd")
QSEM = {"sync": "qS", "gpsimd": "qG", "vector": "qV", "scalar": "qA"}


class Sched:
    def __init__(self):
        self.ops = {e: [] for e in ENGS}
        self.tot = {}

    def emit(self, eng, waits, fn, incs=()):
        for s, n in incs:
            self.tot[s] = self.tot.get(s, 0) + n
        waits = [(s, v) for (s, v) in waits if v > 0]
        incs = list(incs)

        def closure(engine, sems, waits=waits, incs=incs, fn=fn):
            for s, v in waits:
                engine.wait_ge(sems[s], v)
            inst = fn(engine, sems)
            if inst is not None:
                for s, n in incs:
                    inst = inst.then_inc(sems[s], n)
        self.ops[eng].append(closure)

    def now(self, sem):
        return self.tot.get(sem, 0)


def dap(t, offset, dims):
    return bass.AP(t, offset, [list(d) for d in dims])


def fap(t, part0, nparts, col0, dims):
    row = t.shape[1]
    return bass.AP(t, part0 * row + col0, [[row, nparts]] + [list(d) for d in dims])


def build_nc(debug=False):
    nc = bass.Bass("TRN2", target_bir_lowering=False, detect_race_conditions=False)
    S = Sched()

    ap = {}
    def din(name, shape, dt=F32):
        ap[name] = nc.dram_tensor(name, shape, dt, kind="ExternalInput").ap()
    din("xt", [D, TH])
    din("wi_t", [D, DIN])
    din("conv_w", [INTER + 2 * STATE, KCONV])
    din("conv_b", [INTER + 2 * STATE, 1])
    din("dt_bias", [H, 1])
    din("a_log", [H, 1])
    din("d_exp", [1, INTER])
    din("normw_row", [1, INTER])
    din("wo_t", [INTER, D])
    din("wg", [I_FFN, D])
    din("wu", [I_FFN, D])
    din("wd", [D, I_FFN])
    din("wg_t", [D, I_FFN])
    din("wu_t", [D, I_FFN])
    din("wd_t", [I_FFN, D])
    din("clip_x", [1, 1])
    din("clip_down", [1, 1])
    din("isodd", [1, 1])

    out_t = nc.dram_tensor("out_t", [D, T], F32, kind="ExternalOutput").ap()

    dsc = {}
    def scratch(name, shape, dt=F32):
        kind = "ExternalOutput" if (debug and not name.startswith("cc_")) else "Internal"
        dsc[name] = nc.dram_tensor(name, shape, dt, kind=kind).ap()
    scratch("zx_spill", [DIN, TH])
    scratch("xc_spill", [INTER, T])
    scratch("xn_spill", [D, TH])
    scratch("st_spill", [STATE, NCH * INTER])
    scratch("yp_spill", [QCH, NCH * INTER])
    scratch("yn_spill", [INTER, T])
    scratch("h_spill", [I_FFN, T])
    scratch("wgq_spill", [D, I_FFN])
    scratch("wuq_spill", [D, I_FFN])
    scratch("wdq_spill", [I_FFN, D])
    scratch("seg_rows", [NCH, H * QCH])
    scratch("dec_rows", [NCH, INTER])
    scratch("srow_g", [2, I_FFN])
    scratch("srow_u", [2, I_FFN])
    scratch("srow_d", [2, D])
    scratch("scl_dram", [1, 16])
    scratch("dbg_amr", [4, 16])
    scratch("dbg_b3", [QCH, 6 * INTER])
    scratch("dbg_yg", [QCH, NCH * INTER])
    scratch("dbg_yo", [QCH, NCH * INTER])
    scratch("dbg_rm", [QCH, 16])
    scratch("cc_ax_in", [1, 128])
    scratch("cc_ax_out", [1, 128])
    scratch("cc_hx_in", [1, 128])
    scratch("cc_hx_out", [1, 128])
    scratch("cc_cr_in", [STATE, INTER])
    scratch("cc_cr_out", [2 * STATE, INTER])

    with ExitStack() as ctx:
        en = ctx.enter_context
        sbuf = lambda name, shape, dt=F32: en(nc.sbuf_tensor(name, shape, dt))

        XNAR = sbuf("XNAR", [128, DK * TH])
        AR1 = sbuf("AR1", [128, 6144])
        AR2 = sbuf("AR2", [128, 6144])
        AR3 = sbuf("AR3", [128, 4096])
        WIP = sbuf("WIP", [128, 6160])
        CARRY = sbuf("CARRY", [128, INTER])
        II32 = sbuf("II32", [128, 2048], I32)
        ONES = sbuf("ONES", [128, 128])
        IDENT = sbuf("IDENT", [128, 128])
        TRI = sbuf("TRI", [128, 128])
        CW = sbuf("CW", [128, 18 * KCONV])
        CBb = sbuf("CBb", [128, 18])
        SMALL = sbuf("SMALL", [128, 4])
        SCL = sbuf("SCL", [1, 16])
        SCLB = sbuf("SCLB", [128, 16])
        AMX = sbuf("AMX", [128, 64])
        AMR = sbuf("AMR", [1, 16])
        SGTM = sbuf("SGTM", [128, 128])
        BTM = sbuf("BTM", [128, 128])
        RMSB = sbuf("RMSB", [128, 4])
        YNS = sbuf("YNS", [128, 128])
        WSC = sbuf("WSC", [128, 112])
        DTT = sbuf("DTT", [32, T])
        SEG = sbuf("SEG", [32, T])
        ESEG = sbuf("ESEG", [32, T])
        DDEC = sbuf("DDEC", [32, T])

        BCOFF, SEGB_O, MT_O, ZSIL_O, SCT3_O = 0, 4096, 8192, 12288, 14344
        SQ_O, MEAN_O, VAR_O, TMP_O = 0, 4096, 4608, 5120
        XTM_O, CBMT_O, DECB_O, XCR_O = 0, 2048, 2176, 4224
        YNRB_O, HQT_O, OUTST_O, XRES_O = 0, 1024, 2048, 3072
        STST_O, YAS_O, YB_O = 0, 2048, 4096
        SINV_O, SCAL_O = 0, 1408
        DBC_O, NWBC_O = 0, 2048
        GUPS_O, GUPU_O = 0, 2048
        WIB_O, IPST_O = 0, 2048
        XDT_O, XDD_O, WT_O = 0, 2048, 4096

        SP1 = XNAR[0:32, ZSIL_O:ZSIL_O + TH]

        PA = en(nc.psum_tensor("PA", [128, 2048], F32))
        PB = en(nc.psum_tensor("PB", [128, 2048], F32))

        sems = {}
        for name in ("qS", "qG", "qV", "qA", "sT", "sV", "sA", "sP", "cc"):
            sems[name] = en(nc.semaphore(name))
        block = en(nc.Block())

        def dma(engname, out, in_, deps=(), slow=False):
            S.emit(engname, deps,
                   lambda eng, sm, out=out, in_=in_, slow=slow: eng.dma_start(
                       out=out, in_=in_, allow_slow_non_contiguous=slow),
                   incs=[(QSEM[engname], 16)])
            return S.now(QSEM[engname])

        def t_op(fn, deps=(), inc=1):
            S.emit("tensor", deps, fn, incs=[("sT", inc)] if inc else [])
            return S.now("sT")

        def v_op(fn, deps=(), inc=1):
            S.emit("vector", deps, fn, incs=[])
            if inc:
                S.emit("vector", [], lambda eng, sm: eng.engine_nop(),
                       incs=[("sV", inc)])
            return S.now("sV")

        def a_op(fn, deps=(), inc=1):
            S.emit("scalar", deps, fn, incs=[("sA", inc)] if inc else [])
            return S.now("sA")

        def g_op(fn, deps=(), inc=1):
            S.emit("gpsimd", deps, fn, incs=[])
            if inc:
                S.emit("gpsimd", [], lambda eng, sm: eng.engine_nop(),
                       incs=[("sP", inc)])
            return S.now("sP")

        for c in range(18):
            dma("sync", CW[:, c * KCONV:(c + 1) * KCONV], ap["conv_w"][c * 128:(c + 1) * 128, :])
        dma("sync", CBb[:, 0:18], dap(ap["conv_b"].tensor, 0, [(1, 128), (128, 18)]), slow=True)
        dma("sync", SMALL[0:H, 0:1], ap["dt_bias"])
        dma("sync", SMALL[0:H, 1:2], ap["a_log"])
        dma("sync", SCL[:, 8:9], ap["clip_x"])
        dma("sync", SCL[:, 9:10], ap["clip_down"])
        dma("sync", SMALL[:, 2:3], ap["isodd"].broadcast_to([128, 1]))
        dma("sync", AR3[:, DBC_O:DBC_O + INTER], ap["d_exp"].broadcast_to([128, INTER]))
        c_small = dma("sync", AR3[:, NWBC_O:NWBC_O + INTER],
                      ap["normw_row"].broadcast_to([128, INTER]))

        g_op(lambda eng, sm: eng.memset(ONES[:, :], 1.0), inc=0)
        g_op(lambda eng, sm: eng.iota(TRI[:, :], [[1, 128]], channel_multiplier=-1,
                                      allow_small_or_imprecise_dtypes=True), inc=0)
        g_op(lambda eng, sm: eng.memset(AMX[:, :], 0.0), inc=0)
        g_op(lambda eng, sm: eng.memset(CARRY[:, :], 0.0), inc=0)
        g_op(lambda eng, sm: eng.memset(SCL[:, 2:8], 0.0), inc=0)
        g_op(lambda eng, sm: eng.memset(SCL[:, 12:16], 0.0), inc=0)
        g_op(lambda eng, sm: eng.memset(SCL[:, 10:11], 127.0), inc=0)
        c_gc = g_op(lambda eng, sm: eng.memset(SMALL[:, 3:4], 127.0), inc=1)
        v_op(lambda eng, sm: eng.reciprocal(SCL[:, 11:12], SCL[:, 10:11]),
             deps=[("sP", c_gc)], inc=0)
        v_op(lambda eng, sm: eng.reciprocal(SMALL[:, 3:4], SMALL[:, 3:4]), inc=0)
        v_op(lambda eng, sm: eng.tensor_single_scalar(IDENT[:, :], TRI[:, :], 0.0, ALU.is_equal),
             inc=0)
        c_tri = v_op(lambda eng, sm: eng.tensor_single_scalar(TRI[:, :], TRI[:, :], 0.0, ALU.is_ge),
                     inc=1)
        c_a1 = a_op(lambda eng, sm: eng.activation(SMALL[0:H, 1:2], SMALL[0:H, 1:2], AF.Exp,
                                                   bias=0.0, scale=1.0),
                    deps=[("qS", c_small)], inc=1)
        c_const = v_op(lambda eng, sm: eng.tensor_scalar_mul(SMALL[0:H, 1:2], SMALL[0:H, 1:2], -1.0),
                       deps=[("sA", c_a1)], inc=1)

        for k in range(DK):
            dma("sync", XNAR[:, k * TH:(k + 1) * TH], ap["xt"][k * 128:(k + 1) * 128, :])
        c_ldx = S.now("qS")

        p_ln = PA[:, 0:512]
        p_ln2 = PA[:, 512:1024]
        MEANB = AR1[:, MEAN_O:MEAN_O + 512]
        VARB = AR1[:, VAR_O:VAR_O + 512]
        TMPL = AR1[:, TMP_O:TMP_O + 512]
        ttiles = [(i * 512, 512) for i in range(4)] + [(T, HALO)]
        c_lnv = 0
        for ti, (t0, tw) in enumerate(ttiles):
            for k in range(DK):
                t_op(lambda eng, sm, k=k, t0=t0, tw=tw: eng.matmul(
                    p_ln[:, 0:tw], ONES[:, :], XNAR[:, k * TH + t0:k * TH + t0 + tw],
                    start=(k == 0), stop=(k == DK - 1)),
                    deps=([("qS", c_ldx), ("sV", max(c_const, c_lnv)), ("sP", c_gc)]
                          if k == 0 else []),
                    inc=(1 if k == DK - 1 else 0))
            for k in range(DK):
                a_op(lambda eng, sm, k=k, t0=t0, tw=tw: eng.activation(
                    AR1[:, SQ_O + k * 512:SQ_O + k * 512 + tw],
                    XNAR[:, k * TH + t0:k * TH + t0 + tw], AF.Square, bias=0.0, scale=1.0),
                    deps=([("qS", c_ldx), ("sV", c_lnv)] if k == 0 else []),
                    inc=(1 if k == DK - 1 else 0))
            c_sq = S.now("sA")
            for k in range(DK):
                t_op(lambda eng, sm, k=k, tw=tw: eng.matmul(
                    p_ln2[:, 0:tw], ONES[:, :], AR1[:, SQ_O + k * 512:SQ_O + k * 512 + tw],
                    start=(k == 0), stop=(k == DK - 1)),
                    deps=[("sA", c_sq)] if k == 0 else [], inc=(1 if k == DK - 1 else 0))
            t_op(lambda eng, sm: eng.matmul(PB[:, 0:1], ONES[:, :], ONES[:, 0:1],
                                            start=True, stop=True), inc=1)
            t_op(lambda eng, sm: eng.matmul(PB[:, 1:2], ONES[:, :], ONES[:, 0:1],
                                            start=True, stop=True), inc=1)
            c_mm = S.now("sT")
            if tw < 512:
                v_op(lambda eng, sm: eng.memset(p_ln[:, tw:512], 0.0),
                     deps=[("sT", c_mm)], inc=0)
                v_op(lambda eng, sm: eng.memset(p_ln2[:, tw:512], 0.0), inc=0)
            twe = 512
            v_op(lambda eng, sm, twe=twe: eng.tensor_scalar_mul(MEANB[:, 0:twe], p_ln[:, 0:twe], 1.0 / D),
                 deps=[("sT", c_mm)], inc=0)
            v_op(lambda eng, sm, twe=twe: eng.tensor_scalar_mul(VARB[:, 0:twe], p_ln2[:, 0:twe], 1.0 / D), inc=0)
            v_op(lambda eng, sm, twe=twe: eng.tensor_tensor(TMPL[:, 0:twe], MEANB[:, 0:twe], MEANB[:, 0:twe], ALU.mult), inc=0)
            v_op(lambda eng, sm, twe=twe: eng.tensor_tensor(VARB[:, 0:twe], VARB[:, 0:twe], TMPL[:, 0:twe], ALU.subtract), inc=0)
            c_v1 = v_op(lambda eng, sm, twe=twe: eng.tensor_scalar_add(VARB[:, 0:twe], VARB[:, 0:twe], 1e-5), inc=1)
            c_v2 = a_op(lambda eng, sm, twe=twe: eng.activation(TMPL[:, 0:twe], VARB[:, 0:twe], AF.Sqrt,
                                                              bias=0.0, scale=1.0),
                        deps=[("sV", c_v1)], inc=1)
            v_op(lambda eng, sm, twe=twe: eng.reciprocal(AR1[:, SQ_O:SQ_O + twe], TMPL[:, 0:twe]),
                 deps=[("sA", c_v2)], inc=0)
            v_op(lambda eng, sm, twe=twe: eng.tensor_tensor(
                AR1[:, SQ_O:SQ_O + twe], AR1[:, SQ_O:SQ_O + twe], VARB[:, 0:twe], ALU.mult), inc=0)
            v_op(lambda eng, sm, twe=twe: eng.tensor_tensor(
                TMPL[:, 0:twe], TMPL[:, 0:twe], AR1[:, SQ_O:SQ_O + twe], ALU.add), inc=0)
            v_op(lambda eng, sm, twe=twe: eng.tensor_scalar_mul(TMPL[:, 0:twe], TMPL[:, 0:twe], 0.5), inc=0)
            v_op(lambda eng, sm, twe=twe: eng.reciprocal(VARB[:, 0:twe], TMPL[:, 0:twe]), inc=0)
            if ti < 4:
                for k in range(DK):
                    v_op(lambda eng, sm, k=k, t0=t0, tw=tw: eng.tensor_tensor(
                        XNAR[:, k * TH + t0:k * TH + t0 + tw], XNAR[:, k * TH + t0:k * TH + t0 + tw],
                        MEANB[:, 0:tw], ALU.subtract), inc=0)
                    v_op(lambda eng, sm, k=k, t0=t0, tw=tw: eng.tensor_tensor(
                        XNAR[:, k * TH + t0:k * TH + t0 + tw], XNAR[:, k * TH + t0:k * TH + t0 + tw],
                        VARB[:, 0:tw], ALU.mult), inc=0)
                    v_op(lambda eng, sm, k=k, ti=ti, t0=t0, tw=tw: eng.reduce_max(
                        AMX[:, ti * 8 + k:ti * 8 + k + 1],
                        XNAR[:, k * TH + t0:k * TH + t0 + tw], AX.X,
                        apply_absolute_value=True), inc=0)
                c_lnv = v_op(lambda eng, sm: eng.engine_nop(), inc=1)
            else:
                for k in range(DK):
                    c_sub = v_op(lambda eng, sm, k=k, t0=t0, tw=tw: eng.tensor_tensor(
                        XNAR[:, k * TH + t0:k * TH + t0 + tw], XNAR[:, k * TH + t0:k * TH + t0 + tw],
                        MEANB[:, 0:tw], ALU.subtract), inc=1)
                    c_gmul = g_op(lambda eng, sm, k=k, t0=t0, tw=tw: eng.tensor_tensor(
                        XNAR[:, k * TH + t0:k * TH + t0 + tw], XNAR[:, k * TH + t0:k * TH + t0 + tw],
                        VARB[:, 0:tw], ALU.mult),
                        deps=[("sV", c_sub)], inc=1)
                    v_op(lambda eng, sm, k=k, t0=t0, tw=tw: eng.reduce_max(
                        AMX[:, 32 + k:32 + k + 1],
                        XNAR[:, k * TH + t0:k * TH + t0 + tw], AX.X,
                        apply_absolute_value=True),
                        deps=[("sP", c_gmul)], inc=0)
                c_lnv = v_op(lambda eng, sm: eng.engine_nop(), inc=1)
        c_ln = c_lnv

        for k in range(DK):
            dma("gpsimd", dsc["xn_spill"][k * 128:(k + 1) * 128, :], XNAR[:, k * TH:(k + 1) * TH],
                deps=[("sV", c_ln)] if k == 0 else ())
        c_xnsp = S.now("qG")

        c_axr = v_op(lambda eng, sm: eng.reduce_max(AMX[:, 63:64], AMX[:, 0:40], AX.X),
                     deps=[("sV", c_ln)], inc=1)
        dma("gpsimd", dap(dsc["cc_ax_in"].tensor, 0, [(128, 1), (1, 128)]),
            AMX[:, 63:64], deps=[("sV", c_axr)])
        if debug:
            dma("gpsimd", dsc["dbg_amr"][2:3, :], dsc["cc_ax_in"][:, 0:16],
                deps=[("qG", S.now("qG"))])
        c_axq = S.now("qG")
        S.emit("gpsimd", [("qG", c_axq)],
               lambda eng, sm: eng.collective_compute(
                   "AllReduce", ALU.max, replica_groups=[list(range(8))],
                   ins=[dsc["cc_ax_in"].tensor.ap().opt()],
                   outs=[dsc["cc_ax_out"].tensor.ap().opt()]),
               incs=[("cc", 1)])
        c_gm0 = dma("gpsimd", TMPL[0:1, 0:128], dsc["cc_ax_out"][:, :],
                    deps=[("cc", S.now("cc"))])
        c_gm = c_gm0
        c_gmr = v_op(lambda eng, sm: eng.reduce_max(AMR[:, 1:2], TMPL[0:1, 0:128], AX.X),
                     deps=[("qG", c_gm)], inc=1)
        c_sig = a_op(lambda eng, sm: eng.activation(AMR[:, 2:3], SCL[:, 8:9], AF.Sigmoid,
                                                    bias=0.0, scale=1.0),
                     deps=[("qG", c_gm)], inc=1)
        c_g1 = g_op(lambda eng, sm: eng.tensor_tensor(SCL[:, 1:2], AMR[:, 2:3], AMR[:, 1:2], ALU.mult),
                    deps=[("sA", c_sig), ("sV", c_gmr)], inc=1)
        c_s1 = a_op(lambda eng, sm: eng.activation(SCL[:, 4:5], SCL[:, 1:2], AF.Copy,
                                                   bias=0.0, scale=SCL[:, 11:12]),
                    deps=[("sP", c_g1)], inc=1)
        c_g2 = g_op(lambda eng, sm: eng.tensor_copy(SCL[:, 1:2], SCL[:, 4:5]),
                    deps=[("sA", c_s1)], inc=1)
        c_sx = v_op(lambda eng, sm: eng.reciprocal(SCL[:, 0:1], SCL[:, 4:5]),
                    deps=[("sA", c_s1), ("sP", c_g2)], inc=1)
        if debug:
            dma("gpsimd", dsc["dbg_amr"][0:1, :], AMR[:, :], deps=[("sV", c_sx)])
            dma("gpsimd", dsc["dbg_amr"][1:2, :], SCL[:, :], deps=[("sV", c_sx)])
            dma("gpsimd", dsc["dbg_amr"][3:4, :], dsc["cc_ax_out"][:, 0:16], deps=[("sV", c_sx)])

        WIB = lambda par: WIP[:, WIB_O + par * 1024:WIB_O + (par + 1) * 1024]
        IPST = lambda par: WIP[:, IPST_O + par * TH:IPST_O + (par + 1) * TH]
        p_ip = lambda par: PA if par == 0 else PB
        c_ip_mm = {-2: 0, -1: 0}
        c_ip_ev = {-2: 0, -1: 0}
        for m in range(35):
            rows = 32 if m == 34 else 128
            par = m % 2
            for k in range(DK):
                dma("sync", WIB(par)[:, k * 128:k * 128 + rows],
                    ap["wi_t"][k * 128:(k + 1) * 128, m * 128:m * 128 + rows],
                    deps=[("sT", c_ip_mm[m - 2])] if k == 0 else ())
            c_wld = S.now("qS")
            first = True
            for k in range(DK):
                for ti in range(4):
                    t_op(lambda eng, sm, k=k, ti=ti, par=par, rows=rows: eng.matmul(
                        p_ip(par)[0:rows, ti * 512:(ti + 1) * 512],
                        WIB(par)[:, k * 128:k * 128 + rows],
                        XNAR[:, k * TH + HALO + ti * 512:k * TH + HALO + (ti + 1) * 512],
                        start=(k == 0), stop=(k == DK - 1)),
                        deps=([("qS", c_wld), ("sV", max(c_sx, c_ip_ev[m - 2]))]
                              if first else []),
                        inc=(1 if (k == DK - 1 and ti == 3) else 0))
                    first = False
            c_ip_mm[m] = S.now("sT")
            c_ip_ev[m] = v_op(lambda eng, sm, par=par, rows=rows: eng.tensor_copy(
                IPST(par)[0:rows, HALO:TH], p_ip(par)[0:rows, :]),
                deps=[("sT", c_ip_mm[m]), ("qG", S.now("qG"))], inc=1)
            dma("gpsimd", dsc["zx_spill"][m * 128:m * 128 + rows, HALO:TH],
                IPST(par)[0:rows, HALO:TH], deps=[("sV", c_ip_ev[m])])
        p_hA = PA[:, 0:HALO]
        p_hB = PB[:, 0:HALO]
        c_hl_mm = {-2: c_ip_mm[34], -1: c_ip_mm[33]}
        c_hl_ev = {-2: c_ip_ev[34], -1: c_ip_ev[33]}
        for m in range(35):
            rows = 32 if m == 34 else 128
            par = m % 2
            p_h = p_hA if par == 0 else p_hB
            for k in range(DK):
                dma("sync", WIB(par)[:, k * 128:k * 128 + rows],
                    ap["wi_t"][k * 128:(k + 1) * 128, m * 128:m * 128 + rows],
                    deps=[("sT", c_hl_mm[m - 2])] if k == 0 else ())
            c_wld = S.now("qS")
            for k in range(DK):
                t_op(lambda eng, sm, k=k, par=par, rows=rows, p_h=p_h: eng.matmul(
                    p_h[0:rows, :], WIB(par)[:, k * 128:k * 128 + rows],
                    XNAR[:, k * TH:k * TH + HALO],
                    start=(k == 0), stop=(k == DK - 1)),
                    deps=([("qS", c_wld), ("sV", c_hl_ev[m - 2])] if k == 0 else []),
                    inc=(1 if k == DK - 1 else 0))
            c_hl_mm[m] = S.now("sT")
            c_hl_ev[m] = v_op(lambda eng, sm, par=par, rows=rows, p_h=p_h: eng.tensor_copy(
                IPST(par)[0:rows, 0:HALO], p_h[0:rows, :]),
                deps=[("sT", c_hl_mm[m]), ("qG", S.now("qG"))], inc=1)
            dma("gpsimd", dsc["zx_spill"][m * 128:m * 128 + rows, 0:HALO],
                IPST(par)[0:rows, 0:HALO], deps=[("sV", c_hl_ev[m])])
        c_b1g = S.now("qG")
        c_b1 = S.now("sV")

        BC_B = lambda c0=0, w=T: XNAR[:, BCOFF + c0:BCOFF + c0 + w]
        BC_C = lambda c0=0, w=T: XNAR[:, BCOFF + T + c0:BCOFF + T + c0 + w]
        RBK = lambda par: WIP[:, par * 2056:par * 2056 + TH]
        CT1 = XNAR[:, SEGB_O:SEGB_O + T]
        CT2 = XNAR[:, MT_O:MT_O + T]

        c_dtr = dma("sync", SP1[:, :], dsc["zx_spill"][34 * 128:34 * 128 + 32, :],
                    deps=[("qG", c_b1g), ("sV", c_b1)])
        c_d0 = v_op(lambda eng, sm: eng.tensor_scalar(DTT[:, :], SP1[:, HALO:TH],
                                                      SMALL[0:H, 0:1], None, ALU.add),
                    deps=[("qS", c_dtr), ("sV", c_const)], inc=1)
        a_op(lambda eng, sm: eng.activation(SEG[:, :], DTT[:, :], AF.Abs, bias=0.0, scale=1.0),
             deps=[("sV", c_d0)], inc=0)
        a_op(lambda eng, sm: eng.activation(SEG[:, :], SEG[:, :], AF.Exp, bias=0.0, scale=-1.0), inc=0)
        c_spl = a_op(lambda eng, sm: eng.activation(SEG[:, :], SEG[:, :], AF.Ln, bias=1.0, scale=1.0),
                     inc=1)
        v_op(lambda eng, sm: eng.tensor_relu(ESEG[:, :], DTT[:, :]), deps=[("sA", c_spl)], inc=0)
        v_op(lambda eng, sm: eng.tensor_tensor(DTT[:, :], ESEG[:, :], SEG[:, :], ALU.add), inc=0)
        v_op(lambda eng, sm: eng.memset(ESEG[:, :], 0.0), inc=0)
        v_op(lambda eng, sm: eng.tensor_scalar(DDEC[:, :], DTT[:, :], SMALL[0:H, 1:2], None, ALU.mult), inc=0)
        for c in range(NCH):
            v_op(lambda eng, sm, c=c: eng.tensor_tensor_scan(
                SEG[:, c * QCH:(c + 1) * QCH], DDEC[:, c * QCH:(c + 1) * QCH],
                ESEG[:, c * QCH:(c + 1) * QCH], 0.0, ALU.add, ALU.add), inc=0)
        c_seg = v_op(lambda eng, sm: eng.engine_nop(), inc=1)
        for c in range(NCH):
            v_op(lambda eng, sm, c=c: eng.tensor_scalar(
                DDEC[:, c * QCH:(c + 1) * QCH], SEG[:, c * QCH:(c + 1) * QCH],
                SEG[:, c * QCH + QCH - 1:c * QCH + QCH], None, ALU.subtract),
                deps=[("sV", c_seg)] if c == 0 else [], inc=0)
        c_dd1 = v_op(lambda eng, sm: eng.engine_nop(), inc=1)
        a_op(lambda eng, sm: eng.activation(DDEC[:, :], DDEC[:, :], AF.Exp, bias=0.0, scale=-1.0),
             deps=[("sV", c_dd1)], inc=0)
        c_es = a_op(lambda eng, sm: eng.activation(ESEG[:, :], SEG[:, :], AF.Exp, bias=0.0, scale=1.0),
                    inc=1)
        c_dd2 = v_op(lambda eng, sm: eng.tensor_tensor(DDEC[:, :], DDEC[:, :], DTT[:, :], ALU.mult),
                     deps=[("sA", c_es)], inc=1)
        dma("gpsimd", dap(dsc["seg_rows"].tensor, 0, [(QCH, H), (H * QCH, NCH), (1, QCH)]),
            SEG[:, :], deps=[("sV", c_seg)])
        for c in range(NCH):
            v_op(lambda eng, sm, c=c: eng.tensor_copy(
                SP1[:, c * P:(c + 1) * P],
                ESEG[:, c * QCH + QCH - 1:c * QCH + QCH].broadcast_to([H, P])),
                deps=[("sA", c_es), ("sV", c_dd2)] if c == 0 else [], inc=0)
        c_dro = v_op(lambda eng, sm: eng.engine_nop(), inc=1)
        c_rows = dma("gpsimd", dap(dsc["dec_rows"].tensor, 0, [(P, H), (INTER, NCH), (1, P)]),
                     SP1[:, 0:NCH * P], deps=[("sV", c_dro)])

        CT1p = lambda par: XNAR[:, SEGB_O + par * 2048:SEGB_O + par * 2048 + T]
        CT2p = lambda par: XNAR[:, MT_O + par * 2048:MT_O + par * 2048 + T]
        conv_jobs = [(32, 16, "B"), (33, 17, "C")] + [(16 + rc, rc, "x%d" % rc) for rc in range(16)]
        c_cv_ev = {}
        c_cv_sum = {}
        c_cv_dma = {}
        for ji, (srow_, cchunk, tag) in enumerate(conv_jobs):
            par = ji % 2
            c_l = dma("sync", RBK(par)[:, :], dsc["zx_spill"][srow_ * 128:(srow_ + 1) * 128, :],
                      deps=[("qG", c_b1g)] + ([("sA", c_cv_ev[ji - 2])] if ji >= 2
                                              else [("sV", c_b1)]))
            CT1 = CT1p(par)
            CT2 = CT2p(par)
            v_op(lambda eng, sm, par=par, cchunk=cchunk, CT1=CT1: eng.tensor_scalar(
                CT1, RBK(par)[:, HALO - 3:HALO - 3 + T],
                CW[:, cchunk * KCONV:cchunk * KCONV + 1], None, ALU.mult),
                deps=[("qS", c_l), ("qG", c_cv_dma.get(ji - 2, 0)),
                      ("sA", c_cv_ev.get(ji - 2, 0))], inc=0)
            c_v = v_op(lambda eng, sm, par=par, cchunk=cchunk, CT1=CT1: eng.scalar_tensor_tensor(
                CT1, RBK(par)[:, HALO - 2:HALO - 2 + T],
                CW[:, cchunk * KCONV + 1:cchunk * KCONV + 2], CT1, ALU.mult, ALU.add), inc=1)
            CT3 = XNAR[:, SCT3_O:SCT3_O + T]
            g_op(lambda eng, sm, par=par, cchunk=cchunk, CT2=CT2: eng.tensor_tensor(
                CT2, RBK(par)[:, HALO - 1:HALO - 1 + T],
                CW[:, cchunk * KCONV + 2:cchunk * KCONV + 3].broadcast_to([128, T]), ALU.mult),
                deps=[("qS", c_l), ("sV", c_cv_sum.get(ji - 2, 0))], inc=0)
            g_op(lambda eng, sm, par=par, cchunk=cchunk, CT3=CT3: eng.tensor_tensor(
                CT3, RBK(par)[:, HALO:HALO + T],
                CW[:, cchunk * KCONV + 3:cchunk * KCONV + 4].broadcast_to([128, T]), ALU.mult),
                inc=0)
            c_g = g_op(lambda eng, sm, CT2=CT2, CT3=CT3: eng.tensor_tensor(
                CT2, CT2, CT3, ALU.add), inc=1)
            if tag == "B":
                dstap = BC_B()
            elif tag == "C":
                dstap = BC_C()
            else:
                dstap = CT1
            c_sum = v_op(lambda eng, sm, dstap=dstap, CT1=CT1, CT2=CT2: eng.tensor_tensor(
                dstap, CT1, CT2, ALU.add), deps=[("sP", c_g), ("sV", c_v)], inc=1)
            c_cv_sum[ji] = c_sum
            c_cv = a_op(lambda eng, sm, dstap=dstap, cchunk=cchunk: eng.activation(
                dstap, dstap, AF.Silu, bias=CBb[:, cchunk:cchunk + 1], scale=1.0),
                deps=[("sV", c_sum)], inc=1)
            c_cv_ev[ji] = c_cv
            if tag.startswith("x"):
                rc = cchunk
                c_cv_dma[ji] = dma("gpsimd", dsc["xc_spill"][rc * 128:(rc + 1) * 128, :], CT1,
                                   deps=[("sA", c_cv)])
            else:
                c_cv_dma[ji] = S.now("qG")
        c_convg = S.now("qG")
        c_conv = S.now("sA")

        XTM = AR1[:, XTM_O:XTM_O + 2048]
        CBMT = AR1[:, CBMT_O:CBMT_O + 128]
        DECB = AR1[:, DECB_O:DECB_O + 2048]
        XCR = lambda par: AR1[:, XCR_O + par * 128:XCR_O + (par + 1) * 128]
        SEGB = XNAR[:, SEGB_O:SEGB_O + H * QCH]
        MTW = XNAR[:, MT_O:MT_O + H * QCH]
        STST = AR2[:, STST_O:STST_O + 2048]
        YAS = AR2[:, YAS_O:YAS_O + 2048]
        YB = AR2[:, YB_O:YB_O + 2048]
        XDTt = WIP[:, XDT_O:XDT_O + 2048]
        XDDt = WIP[:, XDD_O:XDD_O + 2048]
        p_tr = PB[:, 0:128]
        p_sm = PB[:, 512:640]
        p_cb = PB[:, 1024:1152]
        p_cb2 = PB[:, 1152:1280]
        p_tr2 = PB[:, 1536:1664]

        def headbc(col0):
            return fap(SGTM, 0, 128, col0, [(1, H), (0, P)])

        c_scan = {-1: c_sx}
        c_stdma = {-1: 0}
        c_ypdma = {-1: 0}
        c_ydp = {-1: 0}
        for c in range(NCH):
            cw0 = c * QCH
            deps_tp = [("sV", c_dd2), ("sA", c_conv), ("sV", c_scan[c - 1])]
            t_op(lambda eng, sm, cw0=cw0: eng.transpose(
                p_sm[:, 0:32], ESEG[:, cw0:cw0 + QCH], IDENT[0:32, 0:32]),
                deps=deps_tp, inc=0)
            t_op(lambda eng, sm, cw0=cw0: eng.transpose(
                p_sm[:, 32:64], SEG[:, cw0:cw0 + QCH], IDENT[0:32, 0:32]), inc=0)
            t_op(lambda eng, sm, cw0=cw0: eng.transpose(
                p_sm[:, 64:96], DTT[:, cw0:cw0 + QCH], IDENT[0:32, 0:32]), inc=0)
            c_t1 = t_op(lambda eng, sm, cw0=cw0: eng.transpose(
                p_sm[:, 96:128], DDEC[:, cw0:cw0 + QCH], IDENT[0:32, 0:32]), inc=1)
            c_t2 = t_op(lambda eng, sm, cw0=cw0: eng.transpose(
                p_tr2[:, :], BC_B(cw0, QCH), IDENT[:, :]), inc=1)
            c_cbm = t_op(lambda eng, sm, cw0=cw0: eng.matmul(
                p_cb[:, :], BC_B(cw0, QCH), BC_C(cw0, QCH), start=True, stop=True), inc=1)
            c_sgtm0 = a_op(lambda eng, sm: eng.activation(SGTM[:, :], p_sm[:, :], AF.Copy,
                                                           bias=0.0, scale=1.0),
                           deps=[("sT", c_cbm)], inc=1)
            c_sgtm = v_op(lambda eng, sm: eng.engine_nop(), deps=[("sA", c_sgtm0)], inc=1)
            c_segb = dma("sync", SEGB[:, :],
                         dsc["seg_rows"][c:c + 1, :].broadcast_to([128, H * QCH]),
                         deps=[("qG", c_convg), ("sV", c_scan[c - 1]),
                               ("sP", S.now("sP"))])
            c_btm = v_op(lambda eng, sm: eng.tensor_copy(BTM[:, :], p_tr2[:, :]),
                         deps=[("sT", c_cbm), ("sA", S.now("sA"))], inc=1)
            XSTG = XNAR[:, ZSIL_O:ZSIL_O + 2048]
            c_xl = dma("sync", XSTG,
                       dap(dsc["xc_spill"].tensor, cw0,
                           [(T, 128), (128 * T, 16), (1, QCH)]),
                       deps=[("qG", c_convg), ("sV", S.now("sV"))])
            xslot = lambda rc: (p_tr, p_tr2, p_cb2)[rc % 3]
            c_xt = {}
            c_xev = {-1: 0, -2: 0, -3: 0}
            for rc in range(18):
                if rc < 16:
                    c_xt[rc] = t_op(lambda eng, sm, rc=rc: eng.transpose(
                        xslot(rc)[:, :], XSTG[:, rc * QCH:(rc + 1) * QCH], IDENT[:, :]),
                        deps=([("qS", c_xl), ("sV", c_btm)] if rc == 0 else []) +
                             [("sV", c_xev[rc - 3])], inc=1)
                else:
                    c_xt[rc] = t_op(lambda eng, sm: eng.transpose(
                        p_sm[:, 0:32], SEG[:, 0:QCH], IDENT[0:32, 0:32]), inc=1)
                if rc == 2:
                    c_cbe = v_op(lambda eng, sm: eng.tensor_tensor(
                        CBMT, p_cb[:, :], TRI[:, :], ALU.mult),
                        deps=[("sT", c_xt[1]), ("sV", c_tri)], inc=1)
                if rc >= 2:
                    c_xev[rc - 2] = v_op(lambda eng, sm, rc=rc: eng.tensor_copy(
                        XTM[:, (rc - 2) * QCH:(rc - 1) * QCH], xslot(rc - 2)[:, :]),
                        deps=[("sT", c_xt[rc])], inc=1)
            c_xtm = c_xev[15]
            v_op(lambda eng, sm: eng.tensor_tensor(XDTt, XTM, headbc(64), ALU.mult),
                 deps=[("sV", c_sgtm)], inc=0)
            c_xdt = v_op(lambda eng, sm: eng.tensor_tensor(XDDt, XTM, headbc(96), ALU.mult), inc=1)
            c_m1 = g_op(lambda eng, sm: eng.tensor_tensor(
                MTW, SEGB, fap(SGTM, 0, 128, 32, [(1, H), (0, QCH)]), ALU.subtract),
                deps=[("qS", c_segb), ("sV", c_sgtm), ("sT", c_ydp[c - 1])], inc=1)
            c_m2 = v_op(lambda eng, sm: eng.tensor_scalar_min(MTW, MTW, 0.0),
                        deps=[("sP", c_m1)], inc=1)
            c_m3 = a_op(lambda eng, sm: eng.activation(MTW, MTW, AF.Exp, bias=0.0, scale=1.0),
                        deps=[("sV", c_m2)], inc=1)
            c_m4 = v_op(lambda eng, sm: eng.tensor_tensor(
                MTW, MTW, fap(CBMT.tensor, 0, 128, CBMT_O, [(0, H), (1, QCH)]), ALU.mult),
                deps=[("sA", c_m3), ("sV", c_cbe)], inc=1)
            for h in range(H):
                t_op(lambda eng, sm, h=h: eng.matmul(
                    PA[:, h * P:(h + 1) * P], MTW[:, h * QCH:(h + 1) * QCH],
                    XDTt[:, h * P:(h + 1) * P], start=True, stop=True),
                    deps=[("sV", c_xdt), ("sV", c_m4)] if h == 0 else [],
                    inc=(1 if h == H - 1 else 0))
            c_yd = S.now("sT")
            c_ydp[c] = c_yd
            v_op(lambda eng, sm: eng.tensor_tensor(YB, XTM, AR3[:, DBC_O:DBC_O + INTER], ALU.mult),
                 deps=[("sT", c_yd)], inc=0)
            c_yp = v_op(lambda eng, sm: eng.tensor_tensor(YAS, YB, PA[:, :], ALU.add),
                        deps=[("qG", c_ypdma[c - 1])], inc=1)
            c_ypdma[c] = dma("gpsimd", dsc["yp_spill"][:, c * INTER:(c + 1) * INTER], YAS,
                             deps=[("sV", c_yp)])
            for q in range(4):
                t_op(lambda eng, sm, q=q: eng.matmul(
                    PA[:, q * 512:(q + 1) * 512], BTM[:, :], XDDt[:, q * 512:(q + 1) * 512],
                    start=True, stop=True),
                    deps=[("sV", max(c_yp, c_xdt))] if q == 0 else [],
                    inc=(1 if q == 3 else 0))
            c_st = S.now("sT")
            c_ste = v_op(lambda eng, sm: eng.tensor_copy(STST, PA[:, :]),
                         deps=[("sT", c_st), ("qG", c_stdma[c - 1])], inc=1)
            c_stdma[c] = dma("gpsimd", dsc["st_spill"][:, c * INTER:(c + 1) * INTER], STST,
                             deps=[("sV", c_ste)])
            c_db = dma("sync", DECB, dsc["dec_rows"][c:c + 1, :].broadcast_to([128, INTER]),
                       deps=[("qG", c_rows), ("sV", c_scan[c - 1])])
            v_op(lambda eng, sm: eng.tensor_tensor(YB, CARRY[:, :], DECB, ALU.mult),
                 deps=[("qS", c_db), ("sV", c_ste)], inc=0)
            c_scan[c] = v_op(lambda eng, sm: eng.tensor_tensor(CARRY[:, :], YB, STST, ALU.add), inc=1)

        dma("gpsimd", dsc["cc_cr_in"][:, :], CARRY[:, :], deps=[("sV", c_scan[NCH - 1])])
        c_crq = S.now("qG")
        S.emit("gpsimd", [("qG", c_crq)],
               lambda eng, sm: eng.collective_compute(
                   "AllGather", ALU.bypass,
                   replica_groups=[[0, 1], [2, 3], [4, 5], [6, 7]],
                   ins=[dsc["cc_cr_in"].tensor.ap().opt()],
                   outs=[dsc["cc_cr_out"].tensor.ap().opt()]),
               incs=[("cc", 1)])
        S.emit("gpsimd", [],
               lambda eng, sm: eng.collective_compute(
                   "AllReduce", ALU.max,
                   replica_groups=[[0, 1], [2, 3], [4, 5], [6, 7]],
                   ins=[dsc["cc_ax_in"].tensor.ap().opt()],
                   outs=[dsc["cc_ax_out"].tensor.ap().opt()]),
               incs=[("cc", 1)])
        c_cc2 = S.now("cc")

        WQB = XNAR[:, SEGB_O:SEGB_O + 2816]
        SINVB = lambda w: XNAR[:, MT_O:MT_O + w]
        SCALB = lambda w: XNAR[:, MT_O + 1408:MT_O + 1408 + w]

        def wquant(wrow_ap, wt_ap, spill_ap, srow_ap, nrow_ch, kcols, wsc_col):
            for r in range(nrow_ch):
                c_l = dma("sync", WQB[:, 0:kcols], wrow_ap[r * 128:(r + 1) * 128, :],
                          deps=[("sV", S.now("sV")), ("sT", S.now("sT")),
                                ("qG", S.now("qG"))])
                v_op(lambda eng, sm, r=r, kcols=kcols: eng.reduce_max(
                    WSC[:, wsc_col + r:wsc_col + r + 1], WQB[:, 0:kcols], AX.X,
                    apply_absolute_value=True),
                    deps=[("qS", c_l)] + ([("qG", S.now("qG"))] if r == 0 else []), inc=1)
            c_g5 = g_op(lambda eng, sm, nrow_ch=nrow_ch: eng.tensor_tensor(
                WSC[:, wsc_col + nrow_ch:wsc_col + 2 * nrow_ch],
                WSC[:, wsc_col:wsc_col + nrow_ch],
                SMALL[:, 3:4].broadcast_to([128, nrow_ch]), ALU.mult),
                deps=[("sV", S.now("sV"))], inc=1)
            c_sc = v_op(lambda eng, sm, nrow_ch=nrow_ch: eng.reciprocal(
                WSC[:, wsc_col:wsc_col + nrow_ch],
                WSC[:, wsc_col + nrow_ch:wsc_col + 2 * nrow_ch]),
                deps=[("sP", c_g5)], inc=1)
            ncols_w = nrow_ch * 128
            dma("gpsimd", dap(srow_ap.tensor, 0, [(1, 128), (128, nrow_ch)]),
                WSC[:, wsc_col:wsc_col + nrow_ch], deps=[("sV", c_sc), ("sP", c_g5)], slow=True)
            c_sr = dma("gpsimd", dap(srow_ap.tensor, srow_ap.tensor.shape[1], [(1, 128), (128, nrow_ch)]),
                       WSC[:, wsc_col + nrow_ch:wsc_col + 2 * nrow_ch], deps=(), slow=True)
            ncols = nrow_ch * 128
            halves = [(0, ncols)] if ncols <= 1408 else [(0, 1408), (1408, ncols - 1408)]
            nk = kcols // 128
            for (h0, hw) in halves:
                c_bi = dma("sync", SINVB(hw), srow_ap[0:1, h0:h0 + hw].broadcast_to([128, hw]),
                           deps=[("qG", c_sr), ("sT", S.now("sT")), ("sV", S.now("sV"))])
                c_bs = dma("sync", SCALB(hw), srow_ap[1:2, h0:h0 + hw].broadcast_to([128, hw]),
                           deps=())
                for kk in range(nk):
                    c_l = dma("sync", WQB[:, 0:hw], wt_ap[kk * 128:(kk + 1) * 128, h0:h0 + hw],
                              deps=[("sV", S.now("sV")), ("qG", S.now("qG"))])
                    v_op(lambda eng, sm, hw=hw: eng.tensor_tensor(
                        WQB[:, 0:hw], WQB[:, 0:hw], SINVB(hw), ALU.mult),
                        deps=[("qS", max(c_l, c_bs, c_bi))], inc=0)
                    v_op(lambda eng, sm, hw=hw: eng.tensor_copy(II32[:, 0:hw], WQB[:, 0:hw]), inc=0)
                    v_op(lambda eng, sm, hw=hw: eng.tensor_scalar(
                        WQB[:, 0:hw], II32[:, 0:hw], -128.0, 127.0, ALU.max, ALU.min), inc=0)
                    c_q = v_op(lambda eng, sm, hw=hw: eng.tensor_tensor(
                        WQB[:, 0:hw], WQB[:, 0:hw], SCALB(hw), ALU.mult), inc=1)
                    dma("gpsimd", spill_ap[kk * 128:(kk + 1) * 128, h0:h0 + hw], WQB[:, 0:hw],
                        deps=[("sV", c_q)])

        wquant(ap["wg"], ap["wg_t"], dsc["wgq_spill"], dsc["srow_g"], ICH, D, 0)
        wquant(ap["wu"], ap["wu_t"], dsc["wuq_spill"], dsc["srow_u"], ICH, D, 44)
        wquant(ap["wd"], ap["wd_t"], dsc["wdq_spill"], dsc["srow_d"], DK, I_FFN, 88)
        c_wq = S.now("qG")
        c_wqv = S.now("sV")

        ZSIL = XNAR[:, ZSIL_O:ZSIL_O + 2048]
        SCT3 = XNAR[:, SCT3_O:SCT3_O + 2048]
        c_cri = dma("gpsimd", STST, dsc["cc_cr_out"][0:STATE, :],
                    deps=[("cc", c_cc2), ("sV", c_wqv), ("qG", c_stdma[NCH - 1])])
        c_scan3 = v_op(lambda eng, sm: eng.tensor_scalar(
            CARRY[:, :], STST, SMALL[:, 2:3], None, ALU.mult),
            deps=[("qG", c_cri)], inc=1)

        for c in range(NCH):
            cw0 = c * QCH
            t_op(lambda eng, sm, cw0=cw0: eng.transpose(
                p_sm[:, 0:32], ESEG[:, cw0:cw0 + QCH], IDENT[0:32, 0:32]),
                deps=[("sV", c_scan3)], inc=0)
            for q in range(4):
                t_op(lambda eng, sm, q=q, cw0=cw0: eng.matmul(
                    PA[:, q * 512:(q + 1) * 512], BC_C(cw0, QCH),
                    CARRY[:, q * 512:(q + 1) * 512], start=True, stop=True),
                    deps=[], inc=0)
            c_yo = t_op(lambda eng, sm: eng.transpose(
                p_sm[:, 32:64], SEG[:, 0:QCH], IDENT[0:32, 0:32]), inc=1)
            c_sg0 = a_op(lambda eng, sm: eng.activation(SGTM[:, 0:32], p_sm[:, 0:32], AF.Copy,
                                                         bias=0.0, scale=1.0),
                         deps=[("sT", c_yo)], inc=1)
            c_sg = v_op(lambda eng, sm: eng.engine_nop(), deps=[("sA", c_sg0)], inc=1)
            c_y1 = v_op(lambda eng, sm: eng.tensor_tensor(YAS, PA[:, :], headbc(0), ALU.mult),
                        deps=[("sT", c_yo), ("sV", c_sg)], inc=1)
            if debug:
                dma("gpsimd", dsc["dbg_yo"][:, c * INTER:(c + 1) * INTER], YAS,
                    deps=[("sV", c_y1)])
            c_ypl = dma("sync", YB, dsc["yp_spill"][:, c * INTER:(c + 1) * INTER],
                        deps=[("qG", c_ypdma[c]), ("sV", S.now("sV"))])
            c_y2 = v_op(lambda eng, sm: eng.tensor_tensor(YAS, YAS, YB, ALU.add),
                        deps=[("qS", c_ypl), ("sV", c_y1),
                              ("qG", S.now("qG"))], inc=1)
            ZRB = AR1[:, XTM_O:XTM_O + 2048]
            c_zl = dma("sync", ZRB,
                       dap(dsc["zx_spill"].tensor, HALO + cw0,
                           [(TH, 128), (128 * TH, 16), (1, QCH)]),
                       deps=[("sT", S.now("sT")), ("sV", c_y2)])
            zslot = lambda rc: (p_tr, p_tr2, p_cb2)[rc % 3]
            c_zt = {}
            for rc in range(18):
                if rc < 16:
                    c_zt[rc] = t_op(lambda eng, sm, rc=rc: eng.transpose(
                        zslot(rc)[:, :], ZRB[:, rc * QCH:(rc + 1) * QCH], IDENT[:, :]),
                        deps=([("qS", c_zl)] if rc == 0 else []) +
                             [("sA", S.now("sA") if rc >= 3 else 0)], inc=1)
                else:
                    c_zt[rc] = t_op(lambda eng, sm: eng.transpose(
                        p_sm[:, 0:32], SEG[:, 0:QCH], IDENT[0:32, 0:32]), inc=1)
                if rc >= 2:
                    a_op(lambda eng, sm, rc=rc: eng.activation(
                        ZSIL[:, (rc - 2) * QCH:(rc - 1) * QCH], zslot(rc - 2)[:, :],
                        AF.Silu, bias=0.0, scale=1.0),
                        deps=[("sT", c_zt[rc])], inc=1)
            c_zs = a_op(lambda eng, sm: eng.activation(
                SGTM[0:1, 64:65], SGTM[0:1, 64:65], AF.Copy, bias=0.0, scale=1.0), inc=1)
            c_y3 = v_op(lambda eng, sm: eng.tensor_tensor(YAS, YAS, ZSIL, ALU.mult),
                        deps=[("sA", c_zs), ("sV", c_y2)], inc=1)
            if debug:
                dma("gpsimd", dsc["dbg_yg"][:, c * INTER:(c + 1) * INTER], YAS,
                    deps=[("sV", c_y3)])
                if c == 15:
                    dma("gpsimd", dsc["dbg_b3"][:, 0:INTER], ZSIL, deps=[("sV", c_y3)])
            a_op(lambda eng, sm: eng.activation(SCT3, YAS, AF.Square, bias=0.0, scale=1.0),
                 deps=[("sV", c_y3), ("qG", S.now("qG"))], inc=0)
            c_r1 = a_op(lambda eng, sm: eng.activation(
                SGTM[0:1, 64:65], SGTM[0:1, 64:65], AF.Copy, bias=0.0, scale=1.0), inc=1)
            c_rr = v_op(lambda eng, sm: eng.reduce_sum(RMSB[:, 0:1], SCT3, AX.X),
                        deps=[("sA", c_r1)], inc=1)
            c_r2 = a_op(lambda eng, sm: eng.activation(RMSB[:, 1:2], RMSB[:, 0:1], AF.Copy,
                                                       bias=1e-6, scale=1.0 / INTER),
                        deps=[("sV", c_rr)], inc=1)
            c_r3 = a_op(lambda eng, sm: eng.activation(RMSB[:, 2:3], RMSB[:, 1:2], AF.Sqrt,
                                                       bias=0.0, scale=1.0), inc=1)
            c_r4 = v_op(lambda eng, sm: eng.reciprocal(RMSB[:, 3:4], RMSB[:, 2:3]),
                        deps=[("sA", c_r3)], inc=1)
            c_r5 = a_op(lambda eng, sm: eng.activation(RMSB[:, 0:1], RMSB[:, 3:4], AF.Copy,
                                                       bias=0.0, scale=RMSB[:, 1:2]),
                        deps=[("sV", c_r4)], inc=1)
            c_r6 = v_op(lambda eng, sm: eng.tensor_tensor(RMSB[:, 2:3], RMSB[:, 2:3],
                                                          RMSB[:, 0:1], ALU.add),
                        deps=[("sA", c_r5)], inc=1)
            c_r7 = a_op(lambda eng, sm: eng.activation(RMSB[:, 0:1], RMSB[:, 2:3], AF.Copy,
                                                       bias=0.0, scale=0.5),
                        deps=[("sV", c_r6)], inc=1)
            c_r8 = v_op(lambda eng, sm: eng.reciprocal(RMSB[:, 3:4], RMSB[:, 0:1]),
                        deps=[("sA", c_r7)], inc=1)
            c_r9 = a_op(lambda eng, sm: eng.activation(RMSB[:, 1:2], RMSB[:, 3:4], AF.Copy,
                                                       bias=0.0, scale=1.0),
                        deps=[("sV", c_r8)], inc=1)
            if debug and c == 15:
                dma("gpsimd", dsc["dbg_rm"][:, 0:4], RMSB[:, :], deps=[("sV", S.now("sV"))])
            c_yn = v_op(lambda eng, sm: eng.scalar_tensor_tensor(
                YAS, YAS, RMSB[:, 1:2], AR3[:, NWBC_O:NWBC_O + INTER], ALU.mult, ALU.mult),
                deps=[("qG", S.now("qG")), ("sA", c_r9)], inc=1)
            c_ye = S.now("sV")
            ynslot = lambda rc: (p_tr, p_tr2, p_cb2)[rc % 3]
            c_yt = {}
            for rc in range(18):
                if rc < 16:
                    c_yt[rc] = t_op(lambda eng, sm, rc=rc: eng.transpose(
                        ynslot(rc)[:, :], YAS[:, rc * QCH:(rc + 1) * QCH], IDENT[:, :]),
                        deps=([("sV", c_yn)] if rc == 0 else []) +
                             [("sV", S.now("sV") if rc >= 3 else 0)], inc=1)
                else:
                    c_yt[rc] = t_op(lambda eng, sm: eng.transpose(
                        p_sm[:, 0:32], SEG[:, 0:QCH], IDENT[0:32, 0:32]), inc=1)
                if rc >= 2:
                    c_ye = v_op(lambda eng, sm, rc=rc: eng.tensor_copy(
                        YNS[:, :], ynslot(rc - 2)[:, :]),
                        deps=[("sT", c_yt[rc]), ("qG", S.now("qG"))], inc=1)
                    dma("gpsimd", dsc["yn_spill"][(rc - 2) * 128:(rc - 1) * 128, cw0:cw0 + QCH],
                        YNS[:, :], deps=[("sV", c_ye)])
            c_sl = dma("sync", STST, dsc["st_spill"][:, c * INTER:(c + 1) * INTER],
                       deps=[("qG", c_stdma[c]), ("sV", S.now("sV"))])
            c_db = dma("sync", DECB, dsc["dec_rows"][c:c + 1, :].broadcast_to([128, INTER]),
                       deps=())
            v_op(lambda eng, sm: eng.tensor_tensor(SCT3, CARRY[:, :], DECB, ALU.mult),
                 deps=[("qS", max(c_sl, c_db)), ("sV", c_yn)], inc=0)
            c_scan3 = v_op(lambda eng, sm: eng.tensor_tensor(CARRY[:, :], SCT3, STST, ALU.add), inc=1)
        c_b3g = S.now("qG")
        c_b3v = S.now("sV")

        dma("gpsimd", dsc["scl_dram"][:, :], SCL[:, :], deps=[("sV", c_sx)])
        c_sb = dma("sync", SCLB[:, :], dsc["scl_dram"].tensor.ap().broadcast_to([128, 16]),
                   deps=[("qG", S.now("qG"))])
        for k in range(DK):
            c_xl = dma("sync", XNAR[:, k * 2048:(k + 1) * 2048],
                       dsc["xn_spill"][k * 128:(k + 1) * 128, HALO:TH],
                       deps=[("sV", c_b3v), ("qG", c_b3g)] if k == 0 else ())
        c_xrb = S.now("qS")
        for k in range(DK):
            v_op(lambda eng, sm, k=k: eng.tensor_scalar(
                XNAR[:, k * 2048:(k + 1) * 2048], XNAR[:, k * 2048:(k + 1) * 2048],
                SCLB[:, 0:1], None, ALU.mult),
                deps=[("qS", max(c_xrb, c_sb))] if k == 0 else [], inc=0)
            v_op(lambda eng, sm, k=k: eng.tensor_copy(II32[:, :], XNAR[:, k * 2048:(k + 1) * 2048]), inc=0)
            v_op(lambda eng, sm, k=k: eng.tensor_scalar(
                XNAR[:, k * 2048:(k + 1) * 2048], II32[:, :], -128.0, 127.0, ALU.max, ALU.min), inc=0)
            v_op(lambda eng, sm, k=k: eng.tensor_scalar(
                XNAR[:, k * 2048:(k + 1) * 2048], XNAR[:, k * 2048:(k + 1) * 2048],
                SCLB[:, 1:2], None, ALU.mult), inc=(1 if k == DK - 1 else 0))
        c_xq = S.now("sV")

        GUPS = AR3[:, GUPS_O:GUPS_O + 2048]
        GUPU = AR3[:, GUPU_O:GUPU_O + 2048]
        WT = lambda par: WIP[:, WT_O + par * 1024:WT_O + par * 1024 + 1024]
        c_ge = {-1: 0}
        c_ue = {-1: 0}
        c_gmm = {-1: 0}
        c_umm = {-1: 0}
        c_hdma = {-1: 0}
        for iw in range(ICH):
            par = iw % 2
            for k in range(DK):
                dma("sync", WT(par)[:, k * 128:(k + 1) * 128],
                    dsc["wgq_spill"][k * 128:(k + 1) * 128, iw * 128:(iw + 1) * 128],
                    deps=[("sT", c_umm[iw - 2] if iw >= 2 else 0)] if k == 0 else ())
            c_wg = S.now("qS")
            for k in range(DK):
                for tt in range(4):
                    t_op(lambda eng, sm, k=k, tt=tt, par=par: eng.matmul(
                        PA[:, tt * 512:(tt + 1) * 512], WT(par)[:, k * 128:(k + 1) * 128],
                        XNAR[:, k * 2048 + tt * 512:k * 2048 + (tt + 1) * 512],
                        start=(k == 0), stop=(k == DK - 1)),
                        deps=[("qS", c_wg), ("sV", c_xq), ("sA", c_ge[iw - 1])]
                        if (k == 0 and tt == 0) else [],
                        inc=(1 if (k == DK - 1 and tt == 3) else 0))
            c_gmm[iw] = S.now("sT")
            for k in range(DK):
                dma("sync", WT(par)[:, k * 128:(k + 1) * 128],
                    dsc["wuq_spill"][k * 128:(k + 1) * 128, iw * 128:(iw + 1) * 128],
                    deps=[("sT", c_gmm[iw])] if k == 0 else ())
            c_wu = S.now("qS")
            for k in range(DK):
                for tt in range(4):
                    t_op(lambda eng, sm, k=k, tt=tt, par=par: eng.matmul(
                        PB[:, tt * 512:(tt + 1) * 512], WT(par)[:, k * 128:(k + 1) * 128],
                        XNAR[:, k * 2048 + tt * 512:k * 2048 + (tt + 1) * 512],
                        start=(k == 0), stop=(k == DK - 1)),
                        deps=[("qS", c_wu), ("sV", c_ue[iw - 1])]
                        if (k == 0 and tt == 0) else [],
                        inc=(1 if (k == DK - 1 and tt == 3) else 0))
            c_umm[iw] = S.now("sT")
            c_ge[iw] = a_op(lambda eng, sm: eng.activation(
                GUPS, PA[:, :], AF.Silu, bias=0.0, scale=1.0),
                deps=[("sT", c_gmm[iw]), ("qG", c_hdma[iw - 1])], inc=1)
            v_op(lambda eng, sm: eng.tensor_copy(GUPU, PB[:, :]),
                 deps=[("sT", c_umm[iw])], inc=0)
            c_e2 = v_op(lambda eng, sm: eng.tensor_tensor(GUPS, GUPS, GUPU, ALU.mult),
                        deps=[("sA", c_ge[iw])], inc=1)
            c_ue[iw] = c_e2
            c_hdma[iw] = dma("gpsimd", dsc["h_spill"][iw * 128:(iw + 1) * 128, :], GUPS,
                             deps=[("sV", c_e2)])
            v_op(lambda eng, sm, iw=iw: eng.reduce_max(
                AMX[:, 32 + iw:33 + iw], GUPS, AX.X,
                apply_absolute_value=True), inc=1)
        c_hsp = S.now("qG")
        c_hv = S.now("sV")

        c_hxr = v_op(lambda eng, sm: eng.reduce_max(AMX[:, 62:63], AMX[:, 32:54], AX.X),
                     deps=[("sV", c_hv)], inc=1)
        dma("gpsimd", dap(dsc["cc_hx_in"].tensor, 0, [(128, 1), (1, 128)]),
            AMX[:, 62:63], deps=[("sV", c_hxr)])
        S.emit("gpsimd", [("qG", S.now("qG"))],
               lambda eng, sm: eng.collective_compute(
                   "AllReduce", ALU.max, replica_groups=[list(range(8))],
                   ins=[dsc["cc_hx_in"].tensor.ap().opt()],
                   outs=[dsc["cc_hx_out"].tensor.ap().opt()]),
               incs=[("cc", 1)])
        c_cc3 = S.now("cc")
        c_gmh = dma("gpsimd", GUPS[0:1, 0:128], dsc["cc_hx_out"][:, :], deps=[("cc", c_cc3)])
        c_gmr2 = v_op(lambda eng, sm: eng.reduce_max(AMR[:, 5:6], GUPS[0:1, 0:128], AX.X),
                      deps=[("qG", c_gmh)], inc=1)
        c_sigh = a_op(lambda eng, sm: eng.activation(AMR[:, 6:7], SCL[:, 9:10], AF.Sigmoid,
                                                     bias=0.0, scale=1.0),
                      deps=[("qG", c_gmh)], inc=1)
        c_g3 = g_op(lambda eng, sm: eng.tensor_tensor(SCL[:, 3:4], AMR[:, 6:7], AMR[:, 5:6], ALU.mult),
                    deps=[("sA", c_sigh), ("sV", c_gmr2)], inc=1)
        c_s2 = a_op(lambda eng, sm: eng.activation(SCL[:, 5:6], SCL[:, 3:4], AF.Copy,
                                                   bias=0.0, scale=SCL[:, 11:12]),
                    deps=[("sP", c_g3)], inc=1)
        c_g4 = g_op(lambda eng, sm: eng.tensor_copy(SCL[:, 3:4], SCL[:, 5:6]),
                    deps=[("sA", c_s2)], inc=1)
        c_sh = v_op(lambda eng, sm: eng.reciprocal(SCL[:, 2:3], SCL[:, 5:6]),
                    deps=[("sA", c_s2), ("sP", c_g4)], inc=1)
        dma("gpsimd", dsc["scl_dram"][:, :], SCL[:, :], deps=[("sV", c_sh)])
        c_sb2 = dma("sync", SCLB[:, :], dsc["scl_dram"].tensor.ap().broadcast_to([128, 16]),
                    deps=[("qG", S.now("qG"))])

        YNRB = AR1[:, YNRB_O:YNRB_O + 512]
        YNRB2 = AR1[:, YNRB_O + 512:YNRB_O + 1024]
        HQT = lambda par: AR1[:, HQT_O + par * 512:HQT_O + (par + 1) * 512]
        OUTST = lambda par: AR1[:, OUTST_O + par * 512:OUTST_O + (par + 1) * 512]
        XRES = lambda par: AR1[:, XRES_O + par * 512:XRES_O + (par + 1) * 512]

        pd = lambda d: (PA if d < 4 else PB)[:, (d % 4) * 512:((d % 4) + 1) * 512]
        c_dv = S.now("sV")
        c_dmm_i = {}
        for tt in range(4):
            tc0 = tt * 512
            for i in range(16):
                par = i % 2
                ybuf = YNRB if par == 0 else YNRB2
                c_yl = dma("sync", ybuf, dsc["yn_spill"][i * 128:(i + 1) * 128, tc0:tc0 + 512],
                           deps=[("qG", c_b3g), ("sT", c_dmm_i.get((tt, i - 2), 0))])
                for d in range(DK):
                    c_wl = dma("sync", WT(par)[:, d * 128:(d + 1) * 128],
                               ap["wo_t"][i * 128:(i + 1) * 128, d * 128:(d + 1) * 128],
                               deps=[("sT", c_dmm_i.get((tt, i - 2), 0))] if d == 0 else ())
                for d in range(DK):
                    t_op(lambda eng, sm, d=d, par=par, i=i, ybuf=ybuf: eng.matmul(
                        pd(d), WT(par)[:, d * 128:(d + 1) * 128], ybuf,
                        start=(i == 0), stop=False),
                        deps=[("qS", max(c_wl, c_yl)), ("sV", c_dv)]
                        if d == 0 else [], inc=(1 if d == DK - 1 else 0))
                c_dmm_i[(tt, i)] = S.now("sT")
            for i in range(ICH):
                par = i % 2
                c_hl = dma("sync", GUPS[:, 0:512],
                           dsc["h_spill"][i * 128:(i + 1) * 128, tc0:tc0 + 512],
                           deps=[("qG", c_hsp), ("sV", S.now("sV"))])
                v_op(lambda eng, sm: eng.tensor_scalar(
                    GUPS[:, 0:512], GUPS[:, 0:512], SCLB[:, 2:3], None, ALU.mult),
                    deps=[("qS", max(c_hl, c_sb2))], inc=0)
                v_op(lambda eng, sm: eng.tensor_copy(II32[:, 0:512], GUPS[:, 0:512]), inc=0)
                v_op(lambda eng, sm: eng.tensor_scalar(
                    GUPS[:, 0:512], II32[:, 0:512], -128.0, 127.0, ALU.max, ALU.min), inc=0)
                c_hq = v_op(lambda eng, sm, par=par: eng.tensor_scalar(
                    HQT(par), GUPS[:, 0:512], SCLB[:, 3:4], None, ALU.mult),
                    deps=[("sT", c_dmm_i.get((tt, 16 + i - 2), 0))], inc=1)
                for d in range(DK):
                    c_wl = dma("sync", WT(par)[:, d * 128:(d + 1) * 128],
                               dsc["wdq_spill"][i * 128:(i + 1) * 128, d * 128:(d + 1) * 128],
                               deps=[("sT", c_dmm_i.get((tt, 16 + i - 2), 0))] if d == 0 else ())
                for d in range(DK):
                    t_op(lambda eng, sm, d=d, par=par, i=i: eng.matmul(
                        pd(d), WT(par)[:, d * 128:(d + 1) * 128], HQT(par),
                        start=False, stop=(i == ICH - 1)),
                        deps=[("qS", c_wl), ("sV", c_hq)] if d == 0 else [],
                        inc=(1 if d == DK - 1 else 0))
                c_dmm_i[(tt, 16 + i)] = S.now("sT")
            c_dmm = S.now("sT")
            for d in range(DK):
                par = d % 2
                c_xl = dma("sync", XRES(par),
                           ap["xt"][d * 128:(d + 1) * 128, HALO + tc0:HALO + tc0 + 512],
                           deps=[("sV", S.now("sV"))])
                c_o = v_op(lambda eng, sm, d=d, par=par: eng.tensor_tensor(
                    OUTST(par), pd(d), XRES(par), ALU.add),
                    deps=[("sT", c_dmm), ("qS", c_xl)], inc=1)
                dma("sync", out_t[d * 128:(d + 1) * 128, tc0:tc0 + 512], OUTST(par),
                    deps=[("sV", c_o)])
            c_dv = S.now("sV")
        c_done = S.now("qS")

        S.emit("sync", [("qS", c_done)], lambda eng, sm: None)
        S.emit("gpsimd", [("qG", S.now("qG"))], lambda eng, sm: None)

        @block.sync
        def _(eng):
            for f in S.ops["sync"]:
                f(eng, sems)

        @block.tensor
        def _(eng):
            for f in S.ops["tensor"]:
                f(eng, sems)

        @block.vector
        def _(eng):
            for f in S.ops["vector"]:
                f(eng, sems)

        @block.scalar
        def _(eng):
            for f in S.ops["scalar"]:
                f(eng, sems)

        @block.gpsimd
        def _(eng):
            for f in S.ops["gpsimd"]:
                f(eng, sems)

    return nc


_NC_CACHE = None


def kernel(L0_attn_hidden_states, L0_attn_in_proj_w, L0_attn_conv_w, L0_attn_conv_b,
           L0_attn_dt_bias, L0_attn_A_log, L0_attn_D, L0_attn_norm_weight,
           L0_attn_out_proj_w, L0_ffn_clip_x, L0_ffn_clip_down, L0_ffn_w_gate,
           L0_ffn_w_up, L0_ffn_w_down):
    global _NC_CACHE
    x = np.ascontiguousarray(np.asarray(L0_attn_hidden_states, dtype=np.float32))
    B, L, _ = x.shape
    TPC = L // 2

    if _NC_CACHE is None:
        _NC_CACHE = build_nc()
    nc = _NC_CACHE

    wi_t = np.ascontiguousarray(np.asarray(L0_attn_in_proj_w, np.float32).T)
    wo_t = np.ascontiguousarray(np.asarray(L0_attn_out_proj_w, np.float32).T)
    wg = np.ascontiguousarray(np.asarray(L0_ffn_w_gate, np.float32))
    wu = np.ascontiguousarray(np.asarray(L0_ffn_w_up, np.float32))
    wd = np.ascontiguousarray(np.asarray(L0_ffn_w_down, np.float32))
    wg_t = np.ascontiguousarray(wg.T)
    wu_t = np.ascontiguousarray(wu.T)
    wd_t = np.ascontiguousarray(wd.T)
    conv_w = np.ascontiguousarray(np.asarray(L0_attn_conv_w, np.float32))
    conv_b = np.ascontiguousarray(np.asarray(L0_attn_conv_b, np.float32))[:, None]
    dt_bias = np.ascontiguousarray(np.asarray(L0_attn_dt_bias, np.float32))[:, None]
    a_log = np.ascontiguousarray(np.asarray(L0_attn_A_log, np.float32))[:, None]
    d_vec = np.asarray(L0_attn_D, np.float32)
    d_exp = np.ascontiguousarray(np.repeat(d_vec, P))[None, :]
    normw = np.ascontiguousarray(np.asarray(L0_attn_norm_weight, np.float32))[None, :]
    clip_x = np.asarray(L0_ffn_clip_x, np.float32).reshape(1, 1)
    clip_down = np.asarray(L0_ffn_clip_down, np.float32).reshape(1, 1)

    in_maps = []
    for core in range(8):
        b, half = core // 2, core % 2
        t0 = half * TPC
        xt = np.zeros((D, TH), np.float32)
        xt[:, HALO:] = x[b, t0:t0 + TPC, :].T
        if half == 1:
            xt[:, 0:HALO] = x[b, t0 - HALO:t0, :].T
        in_maps.append({
            "xt": np.ascontiguousarray(xt),
            "wi_t": wi_t, "conv_w": conv_w, "conv_b": conv_b,
            "dt_bias": dt_bias, "a_log": a_log,
            "d_exp": d_exp, "normw_row": normw,
            "wo_t": wo_t, "wg": wg, "wu": wu, "wd": wd,
            "wg_t": wg_t, "wu_t": wu_t, "wd_t": wd_t,
            "clip_x": clip_x, "clip_down": clip_down,
            "isodd": np.asarray([[float(half)]], np.float32),
        })

    res = run_bass_kernel_spmd(nc, in_maps, core_ids=list(range(8)))
    out = np.empty((B, L, D), np.float32)
    for core in range(8):
        b, half = core // 2, core % 2
        t0 = half * TPC
        out[b, t0:t0 + TPC, :] = res.results[core]["out_t"].T
    return out
